# version 02
# baseline (speedup 1.0000x reference)
import numpy as np
import ml_dtypes
import concourse.bass as bass
import concourse.mybir as mybir
from concourse.bass_utils import run_bass_kernel_spmd
from contextlib import ExitStack

F32 = mybir.dt.float32
BF16 = mybir.dt.bfloat16
I32 = mybir.dt.int32
AF = mybir.ActivationFunctionType
ALU = mybir.AluOpType
AXX = mybir.AxisListType.X

NF4 = [-1.0, -0.6961928009986877, -0.5250730514526367, -0.39491748809814453,
       -0.28444138169288635, -0.18477343022823334, -0.09105003625154495, 0.0,
       0.07958029955625534, 0.16093020141124725, 0.24611230194568634,
       0.33791524171829224, 0.44070982933044434, 0.5626170039176941,
       0.6989939212799072, 1.0]

L, D, DIM, DFF, H = 2048, 4096, 512, 2048, 11008
NLAY, EPS = 4, 1e-5
LT, DT, DIMT, FFT, HT = L // 128, D // 128, DIM // 128, DFF // 128, H // 128
NCORE = 8
NLEV = 11
NLEV_RUN = 11
DT_ORDER = list(range(4))
W2_SUBS = [4096, 4096, 2816]
assert sum(W2_SUBS) == H

_SM = {}
_off = 0
def _sm(name, k):
    global _off
    _SM[name] = (_off, k)
    _off += k
_sm("bin", DIMT); _sm("bout", DT); _sm("lnlw", DIMT); _sm("lnlb", DIMT)
for _ly in range(NLAY):
    for _nm in ("l1w", "l1b", "l2w", "l2b", "scl", "b2",
                "phre", "phim", "lcre", "lcim"):
        _sm(f"{_nm}{_ly}", DIMT)
    _sm(f"b1{_ly}", FFT)
NSM = _off


def pack_smalls(ins):
    sm = np.zeros((128, NSM), np.float32)
    def put(name, vec):
        o, k = _SM[name]
        sm[:, o:o + k] = np.ascontiguousarray(vec, np.float32).reshape(k, 128).T
    put("bin", ins["L0_attn_bin"]); put("bout", ins["L0_attn_bout"])
    put("lnlw", ins["L0_attn_lnl_w"]); put("lnlb", ins["L0_attn_lnl_b"])
    for ly in range(NLAY):
        put(f"l1w{ly}", ins["L0_attn_ln1_w"][ly]); put(f"l1b{ly}", ins["L0_attn_ln1_b"][ly])
        put(f"l2w{ly}", ins["L0_attn_ln2_w"][ly]); put(f"l2b{ly}", ins["L0_attn_ln2_b"][ly])
        put(f"scl{ly}", ins["L0_attn_sc_lin"][ly]); put(f"b2{ly}", ins["L0_attn_b2"][ly])
        put(f"phre{ly}", ins["L0_attn_phz_re"][ly]); put(f"phim{ly}", ins["L0_attn_phz_im"][ly])
        put(f"lcre{ly}", ins["L0_attn_lc_re"][ly]); put(f"lcim{ly}", ins["L0_attn_lc_im"][ly])
        put(f"b1{ly}", ins["L0_attn_b1"][ly])
    return sm


class St:

    def __init__(self, nc, ctx, name):
        self.sem = ctx.enter_context(nc.semaphore(name))
        self.n = 0

    def inc(self, instr, k=1):
        instr.then_inc(self.sem, k)
        self.n += k
        return self.n

    def wait(self, eng, val=None):
        eng.wait_ge(self.sem, self.n if val is None else val)


def build_program(upto="full", taps=()):
    taps = set(taps)
    nc = bass.Bass("TRN2", target_bir_lowering=False, detect_race_conditions=False)

    ap = lambda t: t.ap()
    x_in = ap(nc.dram_tensor("x", [L, D], F32, kind="ExternalInput"))
    sm_in = ap(nc.dram_tensor("smalls", [128, NSM], F32, kind="ExternalInput"))
    win_in = ap(nc.dram_tensor("Win", [DIM, D], F32, kind="ExternalInput"))
    w1_in = ap(nc.dram_tensor("aW1", [NLAY, DFF, DIM], F32, kind="ExternalInput"))
    w2_in = ap(nc.dram_tensor("aW2", [NLAY, DIM, DFF], F32, kind="ExternalInput"))
    wout_in = ap(nc.dram_tensor("Wout", [D, DIM], F32, kind="ExternalInput"))
    c1_in = ap(nc.dram_tensor("w1_codes", [H, D], I32, kind="ExternalInput"))
    a1_in = ap(nc.dram_tensor("w1_absmax", [H, D // 64], F32, kind="ExternalInput"))
    c2_in = ap(nc.dram_tensor("w2_codes", [D, H], I32, kind="ExternalInput"))
    a2_in = ap(nc.dram_tensor("w2_absmax", [D, H // 64], F32, kind="ExternalInput"))
    c3_in = ap(nc.dram_tensor("w3_codes", [H, D], I32, kind="ExternalInput"))
    a3_in = ap(nc.dram_tensor("w3_absmax", [H, D // 64], F32, kind="ExternalInput"))
    out_t = ap(nc.dram_tensor("out", [L, D], F32, kind="ExternalOutput"))

    x1scr = ap(nc.dram_tensor("x1scr", [L, D], F32, kind="Internal"))
    ln1scr = ap(nc.dram_tensor("ln1scr", [L, D], BF16, kind="Internal"))
    gscr = ap(nc.dram_tensor("gscr", [8, 128, HT, 256], BF16, kind="Internal"))
    w2cache = ap(nc.dram_tensor("w2cache", [DT, 128, H], BF16, kind="Internal"))
    tap_out = {}
    def declare_tap(name, shape, dt):
        if name in taps:
            tap_out[name] = ap(nc.dram_tensor("tap_" + name, shape, dt,
                                              kind="ExternalOutput"))

    declare_tap("h0", [128, DIMT * L], F32)
    declare_tap("sm", [128, NSM], F32)
    declare_tap("xb", [128, D], F32)
    declare_tap("sqb", [128, D], F32)
    declare_tap("stats", [128, 2 * 7], F32)
    declare_tap("ln0", [L, D], BF16)
    declare_tap("WinT", [128, DT * DIMT * 128], BF16)
    for _t in ("h1", "h2", "h3", "h4"):
        declare_tap(_t, [128, DIMT * L], F32)
    declare_tap("y_scan0", [128, DIMT * L], F32)
    declare_tap("y_ln1", [128, DIMT * L], F32)
    declare_tap("pk", [128, 3 * NLAY * NLEV * DIMT], F32)
    declare_tap("pro", [128, 6 * NLAY * DIMT], F32)
    declare_tap("scanlev", [128, NLEV * L], F32)
    declare_tap("y2_ln2", [128, DIMT * L], BF16)
    declare_tap("attn_fm", [128, D], BF16)
    declare_tap("x1", [L, D], F32)
    declare_tap("ln1", [L, D], BF16)
    declare_tap("w1row", [128, D], BF16)
    declare_tap("g", [HT, 128, L], BF16)
    declare_tap("y2", [L, D], BF16)

    ctx = ExitStack()
    e = ctx.enter_context

    sSm = St(nc, ctx, "sSm")
    sPro = St(nc, ctx, "sPro"); sProS = St(nc, ctx, "sProS")
    sWn = St(nc, ctx, "sWn"); sWt = St(nc, ctx, "sWt")
    sX = St(nc, ctx, "sX")
    sLNd = St(nc, ctx, "sLNd"); sLNs = St(nc, ctx, "sLNs")
    sTR = St(nc, ctx, "sTR")
    sPE = St(nc, ctx, "sPE")
    sEV = St(nc, ctx, "sEV")
    sEVv = St(nc, ctx, "sEVv")
    sCT = St(nc, ctx, "sCT"); sAM = St(nc, ctx, "sAM")
    sLUT = St(nc, ctx, "sLUT"); sWq = St(nc, ctx, "sWq"); sWC = St(nc, ctx, "sWC")
    sG = St(nc, ctx, "sG"); sGL = St(nc, ctx, "sGL"); sW2L = St(nc, ctx, "sW2L")
    sLR = St(nc, ctx, "sLR")
    sX1 = St(nc, ctx, "sX1"); sOut = St(nc, ctx, "sOut")
    sCast = St(nc, ctx, "sCast")
    sBar = St(nc, ctx, "sBar")

    ENGS = ("sync", "vector", "scalar", "tensor", "gpsimd")

    sm = e(nc.sbuf_tensor("smt", [128, NSM], F32))
    def SMC(name, ly=None, i=0):
        o, k = _SM[name if ly is None else f"{name}{ly}"]
        return sm[:, o + i:o + i + 1]
    ones128 = e(nc.sbuf_tensor("ones128", [128, 1], F32))
    ones1r = e(nc.sbuf_tensor("ones1r", [1, 128], F32))
    onesPP = e(nc.sbuf_tensor("onesPP", [128, 128], F32))
    flushb = e(nc.sbuf_tensor("flushb", [128, 64], F32))
    flushs = e(nc.sbuf_tensor("flushs", [128, 64], F32))
    pre_pc = e(nc.sbuf_tensor("pre_pc", [128, NLAY * DIMT], F32))
    pim_pc = e(nc.sbuf_tensor("pim_pc", [128, NLAY * DIMT], F32))
    plcre_pc = e(nc.sbuf_tensor("plcre_pc", [128, NLAY * DIMT], F32))
    plcim_pc = e(nc.sbuf_tensor("plcim_pc", [128, NLAY * DIMT], F32))
    PKS = 1
    pkre = e(nc.sbuf_tensor("pkre", [128, NLAY * NLEV * DIMT * PKS], F32))
    pkim = e(nc.sbuf_tensor("pkim", [128, NLAY * NLEV * DIMT * PKS], F32))
    pknim = e(nc.sbuf_tensor("pknim", [128, NLAY * NLEV * DIMT * PKS], F32))
    r2b = e(nc.sbuf_tensor("r2b", [128, NLAY * DIMT], F32))
    rb = e(nc.sbuf_tensor("rb", [128, NLAY * DIMT], F32))
    thb = e(nc.sbuf_tensor("thb", [128, NLAY * DIMT], F32))
    pt0 = e(nc.sbuf_tensor("pt0", [128, NLAY * DIMT], F32))
    pt1 = e(nc.sbuf_tensor("pt1", [128, NLAY * DIMT], F32))
    pt2 = e(nc.sbuf_tensor("pt2", [128, NLAY * DIMT], F32))
    pt3 = e(nc.sbuf_tensor("pt3", [128, NLAY * DIMT], F32))
    stat = {nm: e(nc.sbuf_tensor(f"st_{nm}", [128, 4], F32))
            for nm in ("sum", "sq", "nm", "var", "rstd", "nmr", "t0", "m2", "f0")}

    pb = [e(nc.psum_tensor(f"pb{i}", [128, 512], F32)) for i in range(8)]
    scope_fm = ExitStack()
    sq2 = tmpA = tmpB = frA = frB = None

    def open_scope_fm():
        nonlocal sq2, tmpA, tmpB, frA, frB
        efm = scope_fm.enter_context
        sq2 = efm(nc.sbuf_tensor("sq2", [128, L], F32))
        tmpA = efm(nc.sbuf_tensor("tmpA", [128, DIMT * 512], F32))
        tmpB = efm(nc.sbuf_tensor("tmpB", [128, DIMT * 512], F32))
        frA = efm(nc.sbuf_tensor("frA", [128, L], F32))
        frB = efm(nc.sbuf_tensor("frB", [128, L], F32))

    NCONST = 8
    ctile = e(nc.sbuf_tensor("ctile", [128, NCONST], F32))
    _consts = {}
    def reg_const(val):
        val = float(val)
        if val in _consts:
            return
        i = len(_consts)
        assert i < NCONST
        _consts[val] = i
        nc.const_aps.register(F32, val, ctile[:, i:i + 1])
    reg_const(-0.5)

    scope_hl = ExitStack()
    hl_bf = scope_hl.enter_context(nc.sbuf_tensor("hl_bf", [128, DIMT * L], BF16))
    scope_h = ExitStack()
    h = scope_h.enter_context(nc.sbuf_tensor("h", [128, DIMT * L], F32))

    block = e(nc.Block())

    def pslice(buf, ly, k=None):
        if k is None:
            return buf[:, ly * DIMT: (ly + 1) * DIMT]
        c0 = (ly * NLEV + k) * DIMT * PKS
        return buf[:, c0:c0 + DIMT * PKS:PKS]

    def body(eng_name):
        return getattr(block, eng_name)

    def barrier():
        tgt = sBar.n + 5
        for en in ENGS:
            @body(en)
            def _(eng, en=en):
                if en == "vector":
                    sBar.inc(eng.tensor_copy(flushb[:], onesPP[:, 0:64]), 1)
                elif en == "scalar":
                    sBar.inc(eng.activation(flushs[:], onesPP[:, 0:64], AF.Copy), 1)
                else:
                    sBar.inc(eng.nop(), 1)
        assert sBar.n == tgt
        for en in ENGS:
            @body(en)
            def _(eng):
                eng.wait_ge(sBar.sem, tgt)

    def emit_tap(name, src_sbuf_ap):
        if name not in tap_out:
            return
        barrier()
        @body("sync")
        def _(sy):
            sOut.inc(sy.dma_start(out=tap_out[name], in_=src_sbuf_ap), 16)
        barrier()

    def ln_tm_stats(v, xap, sqbuf, col, nfeat=D):
        c = slice(col, col + 1)
        v.tensor_tensor(sqbuf, xap, xap, ALU.mult)
        v.reduce_sum(stat["sum"][:, c], xap, axis=AXX)
        v.reduce_sum(stat["sq"][:, c], sqbuf, axis=AXX)
        v.tensor_scalar(stat["nm"][:, c], stat["sum"][:, c],
                        -1.0 / nfeat, None, ALU.mult)
        v.tensor_scalar(stat["t0"][:, c], stat["sq"][:, c],
                        1.0 / nfeat, EPS, ALU.mult, ALU.add)
        v.tensor_tensor(stat["m2"][:, c], stat["nm"][:, c],
                        stat["nm"][:, c], ALU.mult)
        v.tensor_copy(stat["f0"][:, c], stat["nm"][:, c])
        return v.tensor_tensor(stat["var"][:, c], stat["t0"][:, c],
                               stat["m2"][:, c], ALU.subtract)

    def ln_tm_rstd(sc, col):
        sc.activation(stat["rstd"][:, col:col + 1], stat["var"][:, col:col + 1],
                      AF.Ln)
        sc.activation(stat["rstd"][:, col:col + 1], stat["rstd"][:, col:col + 1],
                      AF.Exp, scale=-0.5)
        return sc.mul(stat["nmr"][:, col:col + 1], stat["rstd"][:, col:col + 1],
                      stat["nm"][:, col:col + 1])

    win_scope = ExitStack()
    WinT = win_scope.enter_context(
        nc.sbuf_tensor("WinT", [128, DT * DIMT * 128], BF16))
    WinTv = WinT[:].rearrange("p (k m l) -> p k m l", k=DT, m=DIMT)
    wnat = [win_scope.enter_context(
        nc.sbuf_tensor(f"wnat{i}", [128, D], BF16)) for i in range(2)]

    @body("sync")
    def _(sy):
        sSm.inc(sy.dma_start(out=sm[:], in_=sm_in), 16)

    @body("vector")
    def _(v):
        v.memset(ones128[:], 1.0)
        v.memset(ones1r[:], 1.0)
        v.memset(onesPP[:], 1.0)
        for val, i in _consts.items():
            v.memset(ctile[:, i:i + 1], val)
        sSm.wait(v)
        def REIM(ly):
            o_re, _ = _SM[f"phre{ly}"]; o_im, _ = _SM[f"phim{ly}"]
            return sm[:, o_re:o_re + DIMT], sm[:, o_im:o_im + DIMT]
        for ly in range(NLAY):
            s4 = slice(ly * DIMT, ly * DIMT + DIMT)
            re, im = REIM(ly)
            v.tensor_tensor(pt0[:, s4], re, re, ALU.mult)
        for ly in range(NLAY):
            s4 = slice(ly * DIMT, ly * DIMT + DIMT)
            re, im = REIM(ly)
            v.tensor_tensor(pt1[:, s4], im, im, ALU.mult)
        for ly in range(NLAY):
            s4 = slice(ly * DIMT, ly * DIMT + DIMT)
            insts = v.tensor_tensor(r2b[:, s4], pt0[:, s4], pt1[:, s4], ALU.add)
        sPro.inc(insts, 1)

    @body("scalar")
    def _(sc):
        sPro.wait(sc, 1)
        sc.activation(rb[:], r2b[:], AF.Sqrt)
        sProS.inc(sc.activation(thb[:], rb[:], AF.Tanh), 1)

    @body("vector")
    def _(v):
        sProS.wait(v, 1)
        AL = slice(0, NLAY * DIMT)
        def REIM(ly):
            o_re, _ = _SM[f"phre{ly}"]; o_im, _ = _SM[f"phim{ly}"]
            return sm[:, o_re:o_re + DIMT], sm[:, o_im:o_im + DIMT]
        def LC(ly):
            o_lr, _ = _SM[f"lcre{ly}"]; o_li, _ = _SM[f"lcim{ly}"]
            return sm[:, o_lr:o_lr + DIMT], sm[:, o_li:o_li + DIMT]
        def S4(ly): return slice(ly * DIMT, ly * DIMT + DIMT)
        v.reciprocal(pt1[:], rb[:])
        v.tensor_copy(pt2[:], thb[:])
        v.tensor_copy(pt3[:], rb[:])
        v.tensor_copy(stat["f0"][:, 0:1], ones128[:])
        v.tensor_tensor(pt0[:], pt1[:], pt2[:], ALU.mult)
        v.tensor_copy(pt3[:], thb[:])
        v.tensor_copy(stat["f0"][:, 0:1], ones128[:])
        for ly in range(NLAY):
            v.tensor_tensor(pre_pc[:, S4(ly)], REIM(ly)[0], pt0[:, S4(ly)], ALU.mult)
        for ly in range(NLAY):
            v.tensor_tensor(pim_pc[:, S4(ly)], REIM(ly)[1], pt0[:, S4(ly)], ALU.mult)
        for ly in range(NLAY):
            v.tensor_tensor(pt1[:, S4(ly)], pim_pc[:, S4(ly)], LC(ly)[1], ALU.mult)
        for ly in range(NLAY):
            v.tensor_tensor(pt2[:, S4(ly)], pre_pc[:, S4(ly)], LC(ly)[0], ALU.mult)
        for ly in range(NLAY):
            v.tensor_tensor(plcre_pc[:, S4(ly)], pt2[:, S4(ly)], pt1[:, S4(ly)],
                            ALU.subtract)
        for ly in range(NLAY):
            v.tensor_tensor(pt1[:, S4(ly)], pre_pc[:, S4(ly)], LC(ly)[1], ALU.mult)
        for ly in range(NLAY):
            v.tensor_tensor(pt3[:, S4(ly)], pim_pc[:, S4(ly)], LC(ly)[0], ALU.mult)
        for ly in range(NLAY):
            v.tensor_tensor(plcim_pc[:, S4(ly)], pt3[:, S4(ly)], pt1[:, S4(ly)],
                            ALU.add)
        for ly in range(NLAY):
            v.tensor_copy(pslice(pkre, ly, 0), pre_pc[:, S4(ly)])
        for ly in range(NLAY):
            v.tensor_copy(pslice(pkim, ly, 0), pim_pc[:, S4(ly)])
        for k in range(1, NLEV):
            for ly in range(NLAY):
                a_ = pslice(pkre, ly, k - 1)
                v.tensor_tensor(pt0[:, S4(ly)], a_, a_, ALU.mult)
            for ly in range(NLAY):
                b_ = pslice(pkim, ly, k - 1)
                v.tensor_tensor(pt1[:, S4(ly)], b_, b_, ALU.mult)
            for ly in range(NLAY):
                a_, b_ = pslice(pkre, ly, k - 1), pslice(pkim, ly, k - 1)
                v.tensor_tensor(pslice(pknim, ly, k), a_, b_, ALU.mult)
            for ly in range(NLAY):
                v.tensor_tensor(pslice(pkre, ly, k), pt0[:, S4(ly)],
                                pt1[:, S4(ly)], ALU.subtract)
            for ly in range(NLAY):
                v.tensor_scalar(pslice(pkim, ly, k), pslice(pknim, ly, k),
                                2.0, None, ALU.mult)
        v.tensor_copy(pt0[:], pre_pc[:])
        sPro.inc(v.tensor_scalar(pknim[:], pkim[:], -1.0, None, ALU.mult), 1)

    @body("gpsimd")
    def _(g):
        for mt in range(DIMT):
            if mt >= 2:
                sWt.wait(g, 16 * (mt - 1))
            sWn.inc(g.dma_start(out=wnat[mt % 2][:],
                                in_=win_in[mt * 128:(mt + 1) * 128, :]), 16)

    @body("sync")
    def _(sy):
        for mt in range(DIMT):
            sWn.wait(sy, 16 * (mt + 1))
            sWt.inc(sy.dma_start(out=WinTv[:, :, mt, :], in_=wnat[mt % 2][:],
                                 transpose=True), 16)

    p1_scope = ExitStack()
    ep1 = p1_scope.enter_context
    XB = [ep1(nc.sbuf_tensor(f"XB{i}", [128, D], F32)) for i in range(2)]
    ln0b = [ep1(nc.sbuf_tensor(f"ln0b{i}", [128, D], BF16)) for i in range(2)]
    sqb = ep1(nc.sbuf_tensor("sqb", [128, D], F32))
    pcb = [ep1(nc.sbuf_tensor(f"pcb{i}", [128, DT * 128], BF16)) for i in range(2)]

    NG = LT // 4
    sX0, sLNs0, sLNd0, sTR0, sPE0, sEV0 = sX.n, sLNs.n, sLNd.n, sTR.n, sPE.n, sEV.n
    sEVv0 = sEVv.n

    @body("sync")
    def _(sy):
        for i in range(LT):
            if i >= 2:
                sEVv.wait(sy, sEVv0 + (i - 1))
            sX.inc(sy.dma_start(out=XB[i % 2][:], in_=x_in[i * 128:(i + 1) * 128, :]), 16)
            sEVv.wait(sy, sEVv0 + i + 1)
            if "ln0" in tap_out:
                sOut.inc(sy.dma_start(out=tap_out["ln0"][i * 128:(i + 1) * 128, :],
                                      in_=ln0b[i % 2][:]), 16)
            if i >= 2:
                sPE.wait(sy, sPE0 + (i - 1))
            sTR.inc(sy.dma_start(
                out=pcb[i % 2][:].rearrange("p (k l) -> p k l", k=DT),
                in_=ln0b[i % 2][:], transpose=True), 16)

    @body("vector")
    def _(v):
        for i in range(LT):
            sX.wait(v, sX0 + 16 * (i + 1))
            sLNd.inc(ln_tm_stats(v, XB[i % 2][:], sqb[:], i % 2), 1)
            sLNs.wait(v, sLNs0 + i + 1)
            sEVv.inc(v.tensor_scalar(ln0b[i % 2][:], XB[i % 2][:],
                                     stat["rstd"][:, i % 2:i % 2 + 1],
                                     stat["nmr"][:, i % 2:i % 2 + 1],
                                     ALU.mult, ALU.add), 1)
            if i % 4 == 3:
                gi = i // 4
                sPE.wait(v, sPE0 + 4 * (gi + 1))
                for m in range(DIMT):
                    dst = h[:, m * L + gi * 512: m * L + gi * 512 + 512]
                    sEV.inc(v.tensor_scalar(dst, pb[m][:, :], SMC("bin", i=m),
                                            None, ALU.add), 1)

    @body("scalar")
    def _(sc):
        for i in range(LT):
            sLNd.wait(sc, sLNd0 + i + 1)
            sLNs.inc(ln_tm_rstd(sc, i % 2), 1)

    @body("tensor")
    def _(t):
        sWt.wait(t)
        for i in range(LT):
            sTR.wait(t, sTR0 + 16 * (i + 1))
            if i % 4 == 0 and i >= 4:
                sEV.wait(t, sEV0 + 4 * (i // 4 - 1) + 4)
            c0 = (i % 4) * 128
            for m in range(DIMT):
                for kt in range(DT):
                    mm = t.matmul(pb[m][:, c0:c0 + 128],
                             WinT[:, (kt * DIMT + m) * 128:(kt * DIMT + m) * 128 + 128],
                             pcb[i % 2][:, kt * 128:kt * 128 + 128],
                             start=(kt == 0), stop=(kt == DT - 1))
            sPE.inc(mm, 1)


    barrier()
    if "stats" in tap_out:
        @body("sync")
        def _(sy):
            for j, nm in enumerate(("sum", "sq", "nm", "var", "rstd", "nmr", "t0")):
                sOut.inc(sy.dma_start(out=tap_out["stats"][:, 2 * j:2 * j + 2],
                                      in_=stat[nm][:]), 16)
    emit_tap("xb", XB[1][:])
    emit_tap("sqb", sqb[:])
    emit_tap("sm", sm[:])
    emit_tap("WinT", WinT[:])
    emit_tap("h0", h[:])
    p1_scope.close()
    win_scope.close()

    def finish():
        @body("gpsimd")
        def _(g):
            sOut.wait(g)

    if "pro" in tap_out:
        barrier()
        @body("sync")
        def _(sy):
            W = NLAY * DIMT
            for j, buf in enumerate((r2b, rb, thb, pre_pc, pim_pc, plcre_pc)):
                sOut.inc(sy.dma_start(out=tap_out["pro"][:, j * W:(j + 1) * W],
                                      in_=buf[:]), 16)
        barrier()

    if "pk" in tap_out:
        barrier()
        @body("sync")
        def _(sy):
            W = NLAY * NLEV * DIMT
            sOut.inc(sy.dma_start(out=tap_out["pk"][:, 0:W], in_=pkre[:]), 16)
            sOut.inc(sy.dma_start(out=tap_out["pk"][:, W:2 * W], in_=pkim[:]), 16)
            sOut.inc(sy.dma_start(out=tap_out["pk"][:, 2 * W:3 * W], in_=pknim[:]), 16)
        barrier()

    if upto == "p1":
        finish()
        return nc

    open_scope_fm()
    p2_scope = ExitStack()
    ep2 = p2_scope.enter_context
    ybuf = ep2(nc.sbuf_tensor("ybuf", [128, DIMT * L], F32))
    zim, pgre, pgim = sq2, tmpA, tmpB
    y2buf = ep2(nc.sbuf_tensor("y2buf", [128, DIMT * L], BF16))
    y1buf = ep2(nc.sbuf_tensor("y1buf", [128, FFT * 512], BF16))
    w1T = ep2(nc.sbuf_tensor("w1T", [128, DIMT * FFT * 128], BF16))
    w2T = ep2(nc.sbuf_tensor("w2T", [128, FFT * DIMT * 128], BF16))
    wnat2 = [ep2(nc.sbuf_tensor(f"wnat2{i}", [128, DFF], BF16)) for i in range(2)]
    R_SUM, R_SQ, R_M2 = 0, 32, 64
    R_NM, R_T0, R_VAR = 0, 64, 32

    def hsl(dt_, lq):
        return h[:, dt_ * L + lq * 512: dt_ * L + lq * 512 + 512]

    def emit_ln_fm(ly, out_silu, wname, bname, dst_buf, src_buf=None):
        if src_buf is None:
            src_buf = h
        tv0, vv0, sv0 = sPE.n, sEVv.n, sEV.n

        def ssl(dt_, lq):
            return src_buf[:, dt_ * L + lq * 512: dt_ * L + lq * 512 + 512]

        @body("tensor")
        def _(t):
            sEVv.wait(t, vv0)
            for lq in range(4):
                for dt_ in range(DIMT):
                    mm = t.matmul(pb[4 + lq][0:1, :], ones128[:], ssl(dt_, lq),
                                  start=(dt_ == 0), stop=(dt_ == DIMT - 1))
            sPE.inc(mm, 1)

        @body("vector")
        def _(v):
            sPE.wait(v, tv0 + 1)
            for lq in range(4):
                cp = v.tensor_copy(frA[R_SUM:R_SUM + 1, lq * 512:lq * 512 + 512],
                                   pb[4 + lq][0:1, :])
            sEVv.inc(cp, 1)
            for dt_ in range(DIMT):
                sEVv.inc(v.tensor_tensor(sq2[:], src_buf[:, dt_ * L:(dt_ + 1) * L],
                                         src_buf[:, dt_ * L:(dt_ + 1) * L],
                                         ALU.mult), 1)
                sPE.wait(v, tv0 + 1 + dt_ + 1)

        @body("tensor")
        def _(t):
            sEVv.wait(t, vv0 + 1)
            for dt_ in range(DIMT):
                sEVv.wait(t, vv0 + 1 + dt_ + 1)
                for lq in range(4):
                    mm = t.matmul(pb[4 + lq][0:1, :], ones128[:],
                                  sq2[:, lq * 512:lq * 512 + 512],
                                  start=(dt_ == 0), stop=(dt_ == DIMT - 1))
                sPE.inc(mm, 1)

        @body("vector")
        def _(v):
            sPE.wait(v, tv0 + 5)
            for lq in range(4):
                v.tensor_copy(frA[R_SQ:R_SQ + 1, lq * 512:lq * 512 + 512],
                              pb[4 + lq][0:1, :])
            v.tensor_scalar(frB[R_NM:R_NM + 1, :], frA[R_SUM:R_SUM + 1, :],
                            -1.0 / DIM, None, ALU.mult)
            v.tensor_scalar(frB[R_T0:R_T0 + 1, :], frA[R_SQ:R_SQ + 1, :],
                            1.0 / DIM, EPS, ALU.mult, ALU.add)
            v.tensor_tensor(frA[R_M2:R_M2 + 1, :], frB[R_NM:R_NM + 1, :],
                            frB[R_NM:R_NM + 1, :], ALU.mult)
            v.tensor_copy(stat["f0"][:, 0:1], ones128[:])
            sEVv.inc(v.tensor_tensor(frB[R_VAR:R_VAR + 1, :],
                                     frB[R_T0:R_T0 + 1, :],
                                     frA[R_M2:R_M2 + 1, :], ALU.subtract), 1)

        @body("scalar")
        def _(sc):
            sEVv.wait(sc, vv0 + 6)
            sc.activation(frB[R_VAR:R_VAR + 1, :], frB[R_VAR:R_VAR + 1, :], AF.Ln)
            sEV.inc(sc.activation(frB[R_VAR:R_VAR + 1, :],
                                  frB[R_VAR:R_VAR + 1, :], AF.Exp, scale=-0.5), 1)

        for lq in range(4):
            @body("tensor")
            def _(t, lq=lq):
                sEV.wait(t, sv0 + 1)
                if lq > 0:
                    sEVv.wait(t, vv0 + 6 + lq)
                t.matmul(pb[4][:, :], onesPP[R_NM:R_NM + 1, :],
                         frB[R_NM:R_NM + 1, lq * 512:lq * 512 + 512],
                         start=True, stop=True)
                mm = t.matmul(pb[5][:, :], onesPP[R_VAR:R_VAR + 1, :],
                              frB[R_VAR:R_VAR + 1, lq * 512:lq * 512 + 512],
                              start=True, stop=True)
                sPE.inc(mm, 1)

            @body("vector")
            def _(v, lq=lq):
                sPE.wait(v, tv0 + 5 + lq + 1)
                for dt_ in range(DIMT):
                    v.tensor_tensor(tmpA[:, dt_ * 512:dt_ * 512 + 512],
                                    ssl(dt_, lq), pb[4][:, :], ALU.add)
                for dt_ in range(DIMT):
                    v.tensor_tensor(tmpB[:, dt_ * 512:dt_ * 512 + 512],
                                    tmpA[:, dt_ * 512:dt_ * 512 + 512],
                                    pb[5][:, :], ALU.mult)
                last = None
                if not out_silu:
                    for dt_ in range(DIMT):
                        dst = dst_buf[:, dt_ * L + lq * 512: dt_ * L + lq * 512 + 512]
                        last = v.tensor_scalar(dst, tmpB[:, dt_ * 512:dt_ * 512 + 512],
                                               SMC(wname, ly, dt_),
                                               SMC(bname, ly, dt_),
                                               ALU.mult, ALU.add)
                else:
                    last = v.tensor_copy(stat["f0"][0:1, 0:1], pb[5][0:1, 0:1])
                sEVv.inc(last, 1)

            if out_silu:
                @body("scalar")
                def _(sc, lq=lq):
                    sEVv.wait(sc, vv0 + 6 + lq + 1)
                    for dt_ in range(DIMT):
                        dst = dst_buf[:, dt_ * L + lq * 512: dt_ * L + lq * 512 + 512]
                        act = sc.activation(dst, tmpB[:, dt_ * 512:dt_ * 512 + 512],
                                            AF.Silu, scale=SMC(wname, ly, dt_),
                                            bias=SMC(bname, ly, dt_))
                    sEV.inc(act, 1)

    def emit_scan(ly):
        _lv_done = []
        _lv_out0 = sOut.n
        for dt_ in DT_ORDER:
            @body("vector")
            def _(v, dt_=dt_, first=(dt_ == DT_ORDER[0]), ytgt=None):
                if first:
                    sEV.wait(v)
                yv = ybuf[:, dt_ * L:(dt_ + 1) * L]
                v.memset(zim[:], 0.0)
                v.tensor_copy(stat["f0"][:, 0:1], ones128[:])
                v.tensor_copy(zim[:, 0:1],
                              plcim_pc[:, ly * DIMT + dt_:ly * DIMT + dt_ + 1])
                v.scalar_tensor_tensor(
                    yv[:, 0:1],
                    plcre_pc[:, ly * DIMT + dt_:ly * DIMT + dt_ + 1],
                    1.0, yv[:, 0:1], ALU.mult, ALU.add)
                v.tensor_copy(stat["f0"][:, 0:1], ones128[:])
                v.tensor_copy(stat["f0"][:, 1:2], ones128[:])
                src_re, src_im, dst_re, dst_im = yv, zim[:], pgre[:], pgim[:]
                lv_tap = "scanlev" in tap_out and ly == 0 and dt_ == 1
                for k in range(NLEV_RUN):
                    sft = 1 << k
                    _c0 = ((ly * NLEV + k) * DIMT + dt_) * PKS
                    pre_c = pkre[:, _c0:_c0 + 1]
                    nim_c = pknim[:, _c0:_c0 + 1]
                    pim_c = pkim[:, _c0:_c0 + 1]
                    v.tensor_copy(dst_re[:, 0:sft], src_re[:, 0:sft])
                    v.tensor_copy(dst_im[:, 0:sft], src_im[:, 0:sft])
                    v.scalar_tensor_tensor(dst_re[:, sft:L], src_im[:, 0:L - sft],
                                           nim_c, src_re[:, sft:L],
                                           ALU.mult, ALU.add)
                    v.scalar_tensor_tensor(dst_im[:, sft:L], src_re[:, 0:L - sft],
                                           pim_c, src_im[:, sft:L],
                                           ALU.mult, ALU.add)
                    v.scalar_tensor_tensor(dst_re[:, sft:L], src_re[:, 0:L - sft],
                                           pre_c, dst_re[:, sft:L],
                                           ALU.mult, ALU.add)
                    v.scalar_tensor_tensor(dst_im[:, sft:L], src_im[:, 0:L - sft],
                                           pre_c, dst_im[:, sft:L],
                                           ALU.mult, ALU.add)
                    v.tensor_copy(stat["f0"][:, 0:1], ones128[:])
                    v.tensor_copy(stat["f0"][:, 1:2], ones128[:])
                    if lv_tap:
                        sEVv.inc(v.nop(), 1)
                        _lv_done.append((k, dst_re, sEVv.n))
                        if k >= 1:
                            sOut.wait(v, _lv_out0 + 16 * k)
                    src_re, src_im, dst_re, dst_im = dst_re, dst_im, src_re, src_im
                hv = h[:, dt_ * L:(dt_ + 1) * L]
                sEVv.inc(v.scalar_tensor_tensor(
                    hv, src_re, SMC("scl", ly, dt_), hv, ALU.mult, ALU.add), 1)
            if "scanlev" in tap_out and ly == 0 and dt_ == 1:
                @body("sync")
                def _(sy):
                    for (kk, buf, tgt) in _lv_done:
                        sEVv.wait(sy, tgt)
                        sOut.inc(sy.dma_start(
                            out=tap_out["scanlev"][:, kk * L:(kk + 1) * L],
                            in_=buf), 16)
            if "y_scan0" in tap_out and ly == 0:
                barrier()
                @body("sync")
                def _(sy, dt_=dt_):
                    sOut.inc(sy.dma_start(
                        out=tap_out["y_scan0"][:, dt_ * L:(dt_ + 1) * L],
                        in_=pgre[:]), 16)
                barrier()

    def w1Tsl(kt, ft):
        return w1T[:, (ft * DIMT + kt) * 128:(ft * DIMT + kt) * 128 + 128]

    def w2Tsl(kt, mt):
        return w2T[:, (mt * FFT + kt) * 128:(mt * FFT + kt) * 128 + 128]

    def emit_wload_layer(ly, pe_done_target):
        n0 = sWn.n
        t0 = sWt.n

        @body("gpsimd")
        def _(g):
            if pe_done_target is not None:
                sPE.wait(g, pe_done_target)
            cnt = 0
            for ft in range(FFT):
                if cnt >= 2:
                    sWt.wait(g, t0 + 16 * (cnt - 1))
                sWn.inc(g.dma_start(out=wnat2[cnt % 2][:, 0:DIM],
                                    in_=w1_in[ly, ft * 128:(ft + 1) * 128, :]), 16)
                cnt += 1
            for mt in range(DIMT):
                if cnt >= 2:
                    sWt.wait(g, t0 + 16 * (cnt - 1))
                sWn.inc(g.dma_start(out=wnat2[cnt % 2][:],
                                    in_=w2_in[ly, mt * 128:(mt + 1) * 128, :]), 16)
                cnt += 1

        @body("sync")
        def _(sy):
            cnt = 0
            for ft in range(FFT):
                sWn.wait(sy, n0 + 16 * (cnt + 1))
                dst = w1T[:].rearrange("p (f k l) -> p f k l", f=FFT, k=DIMT)[:, ft, :, :]
                sWt.inc(sy.dma_start(out=dst, in_=wnat2[cnt % 2][:, 0:DIM],
                                     transpose=True), 16)
                cnt += 1
            for mt in range(DIMT):
                sWn.wait(sy, n0 + 16 * (cnt + 1))
                dst = w2T[:].rearrange("p (m k l) -> p m k l", m=DIMT, k=FFT)[:, mt, :, :]
                sWt.inc(sy.dma_start(out=dst, in_=wnat2[cnt % 2][:],
                                     transpose=True), 16)
                cnt += 1

    def emit_mlp(ly):
        for lq in range(4):
            tv0, vv0, sv0 = sPE.n, sEVv.n, sEV.n

            @body("tensor")
            def _(t, lq=lq):
                sEVv.wait(t, vv0)
                for ft in range(FFT):
                    if ft >= 2:
                        sEV.wait(t, sv0 + ft - 1)
                    for kt in range(DIMT):
                        mm = t.matmul(pb[ft % 2][:, :], w1Tsl(kt, ft),
                                      y2buf[:, kt * L + lq * 512: kt * L + lq * 512 + 512],
                                      start=(kt == 0), stop=(kt == DIMT - 1))
                    sPE.inc(mm, 1)

            @body("scalar")
            def _(sc, lq=lq):
                for ft in range(FFT):
                    sPE.wait(sc, tv0 + ft + 1)
                    act = sc.activation(y1buf[:, ft * 512:ft * 512 + 512],
                                        pb[ft % 2][:, :], AF.Silu,
                                        bias=SMC("b1", ly, ft))
                    sEV.inc(act, 1)

            @body("tensor")
            def _(t, lq=lq):
                sEV.wait(t, sv0 + FFT)
                for mt in range(DIMT):
                    if mt >= 2:
                        sEVv.wait(t, vv0 + mt - 1)
                    for kt in range(FFT):
                        mm = t.matmul(pb[2 + mt % 2][:, :], w2Tsl(kt, mt),
                                      y1buf[:, kt * 512:kt * 512 + 512],
                                      start=(kt == 0), stop=(kt == FFT - 1))
                    sPE.inc(mm, 1)

            @body("vector")
            def _(v, lq=lq):
                for mt in range(DIMT):
                    sPE.wait(v, tv0 + FFT + mt + 1)
                    sEVv.inc(v.scalar_tensor_tensor(
                        hsl(mt, lq), pb[2 + mt % 2][:, :], SMC("b2", ly, mt),
                        hsl(mt, lq), ALU.add, ALU.add), 1)

    nlay_emit = NLAY
    stop_after = None
    if upto.startswith("p2"):
        sub = upto[2:]
        if sub in ("ln1", "scan", "ln2", "mlp"):
            nlay_emit, stop_after = 1, sub
    WLOAD_EARLY = False
    for ly in range(nlay_emit):
        if WLOAD_EARLY:
            emit_wload_layer(ly, None if ly == 0 else sPE.n)
        emit_ln_fm(ly, True, "l1w", "l1b", ybuf)
        if stop_after == "ln1" and ly == 0:
            barrier(); emit_tap("y_ln1", ybuf[:]); finish(); return nc
        emit_scan(ly)
        if stop_after == "scan" and ly == 0:
            barrier(); emit_tap("h1", h[:]); finish(); return nc
        emit_ln_fm(ly, False, "l2w", "l2b", y2buf)
        if stop_after == "ln2" and ly == 0:
            barrier(); emit_tap("y2_ln2", y2buf[:]); finish(); return nc
        if not WLOAD_EARLY:
            emit_wload_layer(ly, None if ly == 0 else sPE.n)
        wt_target = sWt.n

        @body("tensor")
        def _(t):
            sWt.wait(t, wt_target)
        emit_mlp(ly)
        if stop_after == "mlp" and ly == 0:
            barrier(); emit_tap("h1", h[:]); finish(); return nc
        if f"h{ly + 1}" in tap_out:
            barrier()
            emit_tap(f"h{ly + 1}", h[:])

    barrier()
    if upto == "p2":
        emit_tap("h4", h[:])
        finish()
        return nc

    p2_scope.close()

    emit_ln_fm(None, False, "lnlw", "lnlb", hl_bf)
    barrier()
    scope_fm.close()
    scope_h.close()

    p3_scope = ExitStack()
    ep3 = p3_scope.enter_context
    WoutT = ep3(nc.sbuf_tensor("WoutT", [128, DT * DIMT * 128], BF16))
    wnat3 = [ep3(nc.sbuf_tensor(f"wnat3{i}", [128, DIM], BF16)) for i in range(2)]
    afm = ep3(nc.sbuf_tensor("afm", [128, DT * 128], BF16))
    atm = ep3(nc.sbuf_tensor("atm", [128, D], BF16))
    XB2 = [ep3(nc.sbuf_tensor(f"XB2{i}", [128, D], F32)) for i in range(2)]
    x1b = [ep3(nc.sbuf_tensor(f"x1b{i}", [128, D], F32)) for i in range(2)]
    sqB = ep3(nc.sbuf_tensor("sqB", [128, D], F32))

    wn0, wt0 = sWn.n, sWt.n

    @body("gpsimd")
    def _(g):
        for mt in range(DT):
            if mt >= 2:
                sWt.wait(g, wt0 + 16 * (mt - 1))
            sWn.inc(g.dma_start(out=wnat3[mt % 2][:],
                                in_=wout_in[mt * 128:(mt + 1) * 128, :]), 16)

    @body("sync")
    def _(sy):
        for mt in range(DT):
            sWn.wait(sy, wn0 + 16 * (mt + 1))
            dst = WoutT[:].rearrange("p (m k l) -> p m k l", m=DT, k=DIMT)[:, mt, :, :]
            sWt.inc(sy.dma_start(out=dst, in_=wnat3[mt % 2][:], transpose=True), 16)

    p3_wt = sWt.n
    tv0, vv0, sv0, x0 = sPE.n, sEVv.n, sEV.n, sX.n
    ld0, ls0, tr0, x10, ou0 = sLNd.n, sLNs.n, sTR.n, sX1.n, sOut.n

    @body("sync")
    def _(sy):
        for i in range(LT):
            if i >= 2:
                sEVv.wait(sy, vv0 + 5 * (i - 1))
            sX.inc(sy.dma_start(out=XB2[i % 2][:],
                                in_=x_in[i * 128:(i + 1) * 128, :]), 16)
            sEVv.wait(sy, vv0 + 5 * i + 1)
            if i >= 1:
                sX1.wait(sy, x10 + 16 * 2 * i)
                if tap_out:
                    sOut.wait(sy, sOut.n)
            sTR.inc(sy.dma_start(out=atm[:].rearrange("p (k l) -> p k l", k=DT),
                                 in_=afm[:], transpose=True), 16)
            sEVv.wait(sy, vv0 + 5 * i + 4)
            sX1.inc(sy.dma_start(out=x1scr[i * 128:(i + 1) * 128, :],
                                 in_=x1b[i % 2][:]), 16)
            if "x1" in tap_out:
                sOut.inc(sy.dma_start(out=tap_out["x1"][i * 128:(i + 1) * 128, :],
                                      in_=x1b[i % 2][:]), 16)
            sEVv.wait(sy, vv0 + 5 * i + 5)
            sX1.inc(sy.dma_start(out=ln1scr[i * 128:(i + 1) * 128, :],
                                 in_=atm[:]), 16)
            if "ln1" in tap_out:
                sOut.inc(sy.dma_start(out=tap_out["ln1"][i * 128:(i + 1) * 128, :],
                                      in_=atm[:]), 16)

    @body("tensor")
    def _(t):
        sWt.wait(t, p3_wt)
        sEVv.wait(t, vv0)
        for i in range(LT):
            for mt in range(DT):
                G = 8 * i + mt // 4
                if mt % 4 == 0 and G >= 2:
                    sEV.wait(t, sv0 + G - 1)
                for kt in range(DIMT):
                    mm = t.matmul(pb[mt % 8][:, 0:128],
                                  WoutT[:, (mt * DIMT + kt) * 128:(mt * DIMT + kt) * 128 + 128],
                                  hl_bf[:, kt * L + i * 128: kt * L + i * 128 + 128],
                                  start=(kt == 0), stop=(kt == DIMT - 1))
                if mt % 4 == 3:
                    sPE.inc(mm, 1)

    @body("vector")
    def _(v):
        for i in range(LT):
            if i >= 1:
                sTR.wait(v, tr0 + 16 * i)
            for mt in range(DT):
                sPE.wait(v, tv0 + 8 * i + mt // 4 + 1)
                ev = v.tensor_scalar(afm[:, mt * 128:(mt + 1) * 128],
                                     pb[mt % 8][:, 0:128], SMC("bout", i=mt),
                                     None, ALU.add)
                if mt % 4 == 3:
                    sEV.inc(ev, 1)
            sEVv.inc(v.tensor_copy(stat["f0"][:, 0:1], ones128[:]), 1)
            sTR.wait(v, tr0 + 16 * (i + 1))
            sX.wait(v, x0 + 16 * (i + 1))
            sLNd.inc(ln_tm_stats(v, atm[:], sqB[:], 0), 1)
            sLNs.wait(v, ls0 + 2 * i + 1)
            if i >= 2:
                sX1.wait(v, x10 + 32 * (i - 2) + 16)
            v.scalar_tensor_tensor(x1b[i % 2][:], atm[:], stat["rstd"][:, 0:1],
                                   XB2[i % 2][:], ALU.mult, ALU.add)
            v.tensor_copy(stat["f0"][:, 0:1], ones128[:])
            sEVv.inc(v.tensor_scalar(x1b[i % 2][:], x1b[i % 2][:], 1.0,
                                     stat["nmr"][:, 0:1], ALU.mult, ALU.add), 3)
            sLNd.inc(ln_tm_stats(v, x1b[i % 2][:], sqB[:], 1), 1)
            sLNs.wait(v, ls0 + 2 * i + 2)
            sEVv.inc(v.tensor_scalar(atm[:], x1b[i % 2][:],
                                     stat["rstd"][:, 1:2], stat["nmr"][:, 1:2],
                                     ALU.mult, ALU.add), 1)

    @body("scalar")
    def _(sc):
        for i in range(LT):
            sLNd.wait(sc, ld0 + 2 * i + 1)
            sLNs.inc(ln_tm_rstd(sc, 0), 1)
            sLNd.wait(sc, ld0 + 2 * i + 2)
            sLNs.inc(ln_tm_rstd(sc, 1), 1)

    barrier()
    p3_scope.close()
    scope_hl.close()
    if upto == "p3":
        finish()
        return nc

    p4a_scope = ExitStack()
    ep4 = p4a_scope.enter_context
    ln1res = ep4(nc.sbuf_tensor("ln1res", [128, LT * DT * 128], BF16))
    ln1v = ln1res[:].rearrange("p (i k l) -> p i k l", i=LT, k=DT)
    RING = 6
    ring = [ep4(nc.sbuf_tensor(f"ring{i}", [128, 2048], BF16)) for i in range(RING)]
    cbuf = [ep4(nc.sbuf_tensor(f"cbuf{i}", [128, 2048], BF16)) for i in range(2)]
    trm = [ep4(nc.sbuf_tensor(f"trm{i}", [128, 2048], BF16)) for i in range(2)]
    accA = ep4(nc.sbuf_tensor("accA", [128, 2048], BF16))
    accB = ep4(nc.sbuf_tensor("accB", [128, 2048], BF16))
    wnq = [ep4(nc.sbuf_tensor(f"wnq{i}", [128, 2048], BF16)) for i in range(2)]
    amb = [ep4(nc.sbuf_tensor(f"amb{i}", [128, 32], F32)) for i in range(2)]
    gbuf = [ep4(nc.sbuf_tensor(f"gbuf{i}", [128, L], BF16)) for i in range(2)]
    silb = [ep4(nc.sbuf_tensor(f"silb{i}", [128, 512], F32)) for i in range(2)]

    sX1_done = sX1.n

    @body("sync")
    def _(sy):
        sX1.wait(sy, sX1_done)
        for i in range(LT):
            sLR.inc(sy.dma_start(out=gbuf[0][:],
                                 in_=ln1scr[i * 128:(i + 1) * 128, 0:2048]), 16)
            sLR.inc(sy.dma_start(out=gbuf[1][:],
                                 in_=ln1scr[i * 128:(i + 1) * 128, 2048:4096]), 16)
            sLR.wait(sy, sLR.n)
            sLR.inc(sy.dma_start(out=ln1v[:, i, 0:16, :], in_=gbuf[0][:],
                                 transpose=True), 16)
            sLR.inc(sy.dma_start(out=ln1v[:, i, 16:32, :], in_=gbuf[1][:],
                                 transpose=True), 16)
            sLR.wait(sy, sLR.n)
    lnres_ready = sLR.n

    units = []
    W2_SUBS6 = [2048, 2048, 2048, 2048, 2048, 768]
    w2queue = [(mt, si, si * 2048, W2_SUBS6[si])
               for mt in range(DT) for si in range(6)]
    w2i = 0
    for m in range(HT):
        units.append(("w1", m, 0, 0, 2048))
        units.append(("w1", m, 1, 2048, 2048))
        units.append(("w3", m, 0, 0, 2048))
        units.append(("w3", m, 1, 2048, 2048))
        want_w2 = ((m + 1) * len(w2queue)) // HT
        while w2i < want_w2:
            mt, si, off, cols = w2queue[w2i]
            units.append(("w2", mt, si, off, cols))
            w2i += 1
    assert w2i == len(w2queue)
    NU = len(units)
    NW2U = len(w2queue)
    last_unit_of_m = {}
    w2_seq = {}
    for u, (kind, m, *_rest) in enumerate(units):
        if kind in ("w1", "w3"):
            last_unit_of_m[m] = u
        else:
            w2_seq[u] = len(w2_seq)
    uidx = {(k, m, hh): u for u, (k, m, hh, off, cols) in enumerate(units)
            if k in ("w1", "w3")}
    NLQB_G = 8

    tvA0, svA0, vvA0 = sPE.n, sEV.n, sEVv.n
    wq0, wc0, lut0, ct0, am0, g0 = sWq.n, sWC.n, sLUT.n, sCT.n, sAM.n, sG.n

    def slot_free_wait(eng, u):
        if u < RING:
            return
        pk, pm, ph, po, pc = units[u - RING]
        if pk in ("w1", "w3"):
            sPE.wait(eng, tvA0 + 2 * (pm + 1))
        else:
            sWC.wait(eng, wc0 + 16 * (w2_seq[u - RING] + 1))

    def unit_code_ap(u):
        kind, m, hh, off, cols = units[u]
        src = {"w1": c1_in, "w3": c3_in, "w2": c2_in}[kind]
        return src[m * 128:(m + 1) * 128, off:off + cols]

    def unit_am_ap(u):
        kind, m, hh, off, cols = units[u]
        src = {"w1": a1_in, "w3": a3_in, "w2": a2_in}[kind]
        return src[m * 128:(m + 1) * 128, off // 64: off // 64 + cols // 64]

    cast0 = sCast.n
    gstart = {m: uidx[("w1", m, 0)] for m in range(HT)}
    CODES_VIA_SYNC = False

    def wait_codes_ready(eng, u):
        if CODES_VIA_SYNC:
            sCast.wait(eng, cast0 + u + 1)
        else:
            sCT.wait(eng, ct0 + 16 * (u + 1))

    @body("gpsimd")
    def _(g):
        sX1.wait(g, sX1_done)
        for u in range(NU):
            if u >= 2:
                sLUT.wait(g, lut0 + u - 1)
            if not CODES_VIA_SYNC:
                sCT.inc(g.dma_start(out=cbuf[u % 2][:, 0:units[u][4]],
                                    in_=unit_code_ap(u)), 16)
            sAM.inc(g.dma_start(out=amb[u % 2][:, 0:units[u][4] // 64],
                                in_=unit_am_ap(u)), 16)

    @body("vector")
    def _(v):
        gate_m = 0

        def emit_gates_for(mm_):
            nonlocal gate_m
            while gate_m <= mm_:
                mg = gate_m
                sPE.wait(v, tvA0 + 2 * mg + 2)
                if mg >= 2:
                    sG.wait(v, g0 + 16 * 8 * (mg - 1))
                for lq in range(4):
                    sEV.wait(v, svA0 + 8 * mg + lq + 1)
                    gi = v.tensor_tensor(gbuf[mg % 2][:, lq * 512:lq * 512 + 512],
                                         silb[lq % 2][:], pb[4 + lq][:, :],
                                         ALU.mult)
                    sEVv.inc(gi, 1)
                gate_m += 1

        for u in range(NU):
            kind, m, hh, off, cols = units[u]
            nblk = cols // 64
            wait_codes_ready(v, u)
            sAM.wait(v, am0 + 16 * (u + 1))
            if u >= 2:
                sWq.wait(v, wq0 + 16 * (u - 1))
            cb = cbuf[u % 2][:, 0:cols]
            v.tensor_scalar(accA[:, 0:cols], cb, 0.0, NF4[0], ALU.is_equal, ALU.mult)
            v.tensor_scalar(trm[0][:, 0:cols], cb, 1.0, NF4[1], ALU.is_equal, ALU.mult)
            v.tensor_scalar(trm[1][:, 0:cols], cb, 2.0, NF4[2], ALU.is_equal, ALU.mult)
            acc = accA
            for j in range(3, 18):
                na = accB if acc is accA else accA
                if j >= 16:
                    v.tensor_copy(stat["f0"][:, 0:1], ones128[:])
                v.tensor_tensor(na[:, 0:cols], acc[:, 0:cols],
                                trm[(j - 3) % 2][:, 0:cols], ALU.add)
                acc = na
                if j <= 15:
                    v.tensor_scalar(trm[(j - 3) % 2][:, 0:cols], cb, float(j),
                                    NF4[j], ALU.is_equal, ALU.mult)
            v.tensor_copy(stat["f0"][:, 0:1], ones128[:])
            v.tensor_copy(stat["f0"][:, 1:2], ones128[:])
            lu = v.tensor_tensor(
                wnq[u % 2][:, 0:cols].rearrange("p (c r) -> p c r", c=nblk),
                acc[:, 0:cols].rearrange("p (c r) -> p c r", c=nblk),
                amb[u % 2][:, 0:nblk].unsqueeze(-1).broadcast_to([128, nblk, 64]),
                ALU.mult)
            sLUT.inc(lu, 1)
            if kind == "w3" and hh == 1 and m >= 1:
                emit_gates_for(m - 1)
        emit_gates_for(HT - 1)

    @body("sync")
    def _(sy):
        def g_store(mg):
            sEVv.wait(sy, vvA0 + 4 * (mg + 1))
            for lq2 in range(NLQB_G):
                sG.inc(sy.dma_start(
                    out=gscr[lq2, :, mg, :],
                    in_=gbuf[mg % 2][:, lq2 * 256:(lq2 + 1) * 256]), 16)

        if CODES_VIA_SYNC:
            sCT.inc(sy.dma_start(out=ci32[:, 0:units[0][4]],
                                 in_=unit_code_ap(0)), 16)
        for u in range(NU):
            kind, m, hh, off, cols = units[u]
            if CODES_VIA_SYNC and u + 1 < NU:
                sCast.wait(sy, cast0 + u + 1)
                sCT.inc(sy.dma_start(out=ci32[:, 0:units[u + 1][4]],
                                     in_=unit_code_ap(u + 1)), 16)
            sLUT.wait(sy, lut0 + u + 1)
            slot_free_wait(sy, u)
            sWq.inc(sy.dma_start(
                out=ring[u % RING][:].rearrange("p (k l) -> p k l", k=16)[:, 0:cols // 128, :],
                in_=wnq[u % 2][:, 0:cols], transpose=True), 16)
            if kind == "w2":
                sWq.wait(sy, sWq.n)
                sWC.inc(sy.dma_start(out=w2cache[m, :, off:off + cols],
                                     in_=ring[u % RING][:, 0:cols]), 16)
            if kind == "w3" and hh == 1 and m >= 1:
                g_store(m - 1)
        g_store(HT - 1)

    @body("tensor")
    def _(t):
        sLR.wait(t, lnres_ready)
        for m in range(HT):
            sWq.wait(t, wq0 + 16 * (last_unit_of_m[m] + 1))
            if m >= 1:
                sEV.wait(t, svA0 + 8 * (m - 1) + 8)
                sEVv.wait(t, vvA0 + 4 * m)
            for mat in ("w1", "w3"):
                pbase = 0 if mat == "w1" else 4
                s0_ = ring[uidx[(mat, m, 0)] % RING]
                s1_ = ring[uidx[(mat, m, 1)] % RING]
                for kt in range(DT):
                    lhsT = (s0_ if kt < 16 else s1_)[:, (kt % 16) * 128:(kt % 16) * 128 + 128]
                    for lq in range(4):
                        mm_i = t.matmul(pb[pbase + lq][:, :], lhsT,
                                        ln1v[:, 4 * lq:4 * lq + 4, kt, :],
                                        start=(kt == 0), stop=(kt == DT - 1))
                sPE.inc(mm_i, 1)

    @body("scalar")
    def _(sc):
        ucur = 0

        def emit_casts_upto(ulim):
            nonlocal ucur
            while ucur < ulim:
                u = ucur
                sCT.wait(sc, ct0 + 16 * (u + 1))
                if u >= 2:
                    sLUT.wait(sc, lut0 + u - 1)
                ca = sc.activation(cbuf[u % 2][:, 0:units[u][4]],
                                   ci32[:, 0:units[u][4]], AF.Copy)
                sCast.inc(ca, 1)
                ucur += 1

        for m in range(HT):
            if CODES_VIA_SYNC:
                emit_casts_upto(gstart[m + 2] if m + 2 < HT else NU)
            sPE.wait(sc, tvA0 + 2 * m + 1)
            for lq in range(4):
                if 4 * m + lq >= 2:
                    sEVv.wait(sc, vvA0 + 4 * m + lq - 1)
                si = sc.activation(silb[lq % 2][:], pb[lq][:, :], AF.Silu)
                sEV.inc(si, 1)
            sEV.inc(sc.nop(), 4)
        if CODES_VIA_SYNC:
            emit_casts_upto(NU)

    barrier()
    if "g" in tap_out:
        @body("sync")
        def _(sy):
            sG.wait(sy, g0 + 16 * 8 * HT)
            for lq2 in range(NLQB_G):
                sOut.inc(sy.dma_start(
                    out=tap_out["g"][:, :, lq2 * 256:(lq2 + 1) * 256],
                    in_=gscr[lq2].rearrange("p k c -> k p c")), 16)
        barrier()
    p4a_scope.close()
    if upto == "p4a":
        finish()
        return nc

    p4b_scope = ExitStack()
    ep4b = p4b_scope.enter_context
    LQB = 256
    NLQB = L // LQB
    gsl = ep4b(nc.sbuf_tensor("gsl", [128, HT * LQB], BF16))
    gslv = gsl[:].rearrange("p (k c) -> p k c", k=HT)
    w2tb = [ep4b(nc.sbuf_tensor(f"w2tb{i}", [128, H], BF16)) for i in range(2)]
    ffm = ep4b(nc.sbuf_tensor("ffm", [128, 2 * DT * 128], BF16))
    ffmv = ffm[:].rearrange("p (j m l) -> p j m l", j=2, m=DT)
    ftm = [ep4b(nc.sbuf_tensor(f"ftm{i}", [128, D], BF16)) for i in range(2)]
    sqB2 = ep4b(nc.sbuf_tensor("sqB2", [128, D], F32))
    x1t = [ep4b(nc.sbuf_tensor(f"x1t{i}", [128, D], F32)) for i in range(2)]
    otb = ep4b(nc.sbuf_tensor("otb", [128, D], F32))

    gA_done = sG.n
    wc_done = sWC.n
    gl0, w2l0 = sGL.n, sW2L.n
    tvB0, svB0, vvB0 = sPE.n, sEV.n, sEVv.n
    trB0, ldB0, lsB0, ouB0 = sTR.n, sLNd.n, sLNs.n, sOut.n
    x1l0 = sX1.n

    @body("sync")
    def _(sy):
        sG.wait(sy, gA_done)
        sWC.wait(sy, wc_done)
        for lq in range(NLQB):
            if lq >= 1:
                sPE.wait(sy, tvB0 + DT * lq)
            sGL.inc(sy.dma_start(out=gsl[:], in_=gscr[lq]), 16)
            for j in range(2):
                i_ = lq * 2 + j
                if i_ >= 2:
                    sEVv.wait(sy, vvB0 + 3 * (i_ - 1))
                sX1.inc(sy.dma_start(out=x1t[i_ % 2][:],
                                     in_=x1scr[i_ * 128:(i_ + 1) * 128, :]), 16)
            for mt in range(DT):
                if lq > 0 or mt >= 2:
                    sEV.wait(sy, svB0 + DT * lq + mt - 1)
                sW2L.inc(sy.dma_start(out=w2tb[mt % 2][:],
                                      in_=w2cache[mt, :, :]), 16)
            for j in range(2):
                i_ = lq * 2 + j
                sEV.wait(sy, svB0 + DT * (lq + 1))
                if i_ >= 2:
                    sEVv.wait(sy, vvB0 + 3 * (i_ - 1))
                sTR.inc(sy.dma_start(
                    out=ftm[j % 2][:].rearrange("p (k l) -> p k l", k=DT),
                    in_=ffm[:, j * D:(j + 1) * D], transpose=True), 16)
                sEVv.wait(sy, vvB0 + 3 * i_ + 3)
                if "y2" in tap_out:
                    sOut.inc(sy.dma_start(
                        out=tap_out["y2"][i_ * 128:(i_ + 1) * 128, :],
                        in_=ftm[j % 2][:]), 16)
                sOut.inc(sy.dma_start(out=out_t[i_ * 128:(i_ + 1) * 128, :],
                                      in_=otb[:]), 16)

    @body("tensor")
    def _(t):
        for lq in range(NLQB):
            sGL.wait(t, gl0 + 16 * (lq + 1))
            for mt in range(DT):
                sW2L.wait(t, w2l0 + 16 * (DT * lq + mt + 1))
                if lq > 0 or mt >= 2:
                    sEV.wait(t, svB0 + DT * lq + mt - 1)
                for kt in range(HT):
                    mm_i = t.matmul(pb[mt % 2][:, 0:LQB],
                                    w2tb[mt % 2][:, kt * 128:(kt + 1) * 128],
                                    gslv[:, kt, :],
                                    start=(kt == 0), stop=(kt == HT - 1))
                sPE.inc(mm_i, 1)

    @body("scalar")
    def _(sc):
        for lq in range(NLQB):
            for mt in range(DT):
                if mt == 0 and lq >= 1:
                    sTR.wait(sc, trB0 + 16 * 2 * lq)
                sPE.wait(sc, tvB0 + DT * lq + mt + 1)
                ev = sc.activation(ffmv[:, :, mt, :], pb[mt % 2][:, 0:LQB],
                                   AF.Copy)
                sEV.inc(ev, 1)
            for j in range(2):
                i_ = lq * 2 + j
                sLNd.wait(sc, ldB0 + i_ + 1)
                sLNs.inc(ln_tm_rstd(sc, 2 + (i_ % 2)), 1)

    @body("vector")
    def _(v):
        for lq in range(NLQB):
            for j in range(2):
                i_ = lq * 2 + j
                sTR.wait(v, trB0 + 16 * (i_ + 1))
                sLNd.inc(ln_tm_stats(v, ftm[j % 2][:], sqB2[:], 2 + (i_ % 2)), 1)
                sLNs.wait(v, lsB0 + i_ + 1)
                sX1.wait(v, x1l0 + 16 * (i_ + 1))
                if i_ >= 1:
                    sOut.wait(v, ouB0 + 16 * i_)
                c_ = 2 + (i_ % 2)
                v.scalar_tensor_tensor(otb[:], ftm[j % 2][:],
                                       stat["rstd"][:, c_:c_ + 1],
                                       x1t[i_ % 2][:], ALU.mult, ALU.add)
                sEVv.inc(v.nop(), 2)
                v.tensor_copy(stat["f0"][:, 0:1], ones128[:])
                sEVv.inc(v.tensor_scalar(otb[:], otb[:], 1.0,
                                         stat["nmr"][:, c_:c_ + 1],
                                         ALU.mult, ALU.add), 1)

    barrier()
    p4b_scope.close()
    finish()
    return nc


def shard_inputs(inputs):
    sm = pack_smalls(inputs)
    base = dict(
        smalls=sm,
        Win=np.ascontiguousarray(inputs["L0_attn_Win"]),
        aW1=np.ascontiguousarray(inputs["L0_attn_W1"]),
        aW2=np.ascontiguousarray(inputs["L0_attn_W2"]),
        Wout=np.ascontiguousarray(inputs["L0_attn_Wout"]),
        w1_codes=np.ascontiguousarray(inputs["L0_ffn_w1_codes"]),
        w1_absmax=np.ascontiguousarray(inputs["L0_ffn_w1_absmax"]),
        w2_codes=np.ascontiguousarray(inputs["L0_ffn_w2_codes"]),
        w2_absmax=np.ascontiguousarray(inputs["L0_ffn_w2_absmax"]),
        w3_codes=np.ascontiguousarray(inputs["L0_ffn_w3_codes"]),
        w3_absmax=np.ascontiguousarray(inputs["L0_ffn_w3_absmax"]),
    )
    x = np.ascontiguousarray(inputs["L0_attn_x"], np.float32)
    return [dict(base, x=x[b]) for b in range(NCORE)]


def kernel(**inputs):
    nc = build_program(upto="full")
    in_maps = shard_inputs(inputs)
    res = run_bass_kernel_spmd(nc, in_maps, core_ids=list(range(NCORE)))
    return np.stack([res.results[b]["out"] for b in range(NCORE)], axis=0)
